# version 01 (final)
# baseline (speedup 1.0000x reference)
import numpy as np
import concourse.bass as bass
import concourse.mybir as mybir
from concourse.bass_utils import run_bass_kernel_spmd
from contextlib import ExitStack

F32 = mybir.dt.float32
BF16 = mybir.dt.bfloat16
U8 = mybir.dt.uint8
I32 = mybir.dt.int32
AF = mybir.ActivationFunctionType
ALU = mybir.AluOpType

D = 2048
ND = 16
NI = 64
TC = 2048
GF = 512
NGF = TC // GF
GA = 1024
NGA = TC // GA
EPS = 1e-5
SCALE = float(np.float32(1.0) / np.sqrt(np.float32(2048.0)))
INV_D = float(np.float32(1.0) / np.float32(2048.0))
WTILE = 2
NTI = NI // WTILE


def build_nc(dbg="full"):
    nc = bass.Bass("TRN2", target_bir_lowering=False, debug=False)

    xin = nc.dram_tensor("xin", [D, TC], F32, kind="ExternalInput").ap()
    wq = nc.dram_tensor("wq", [2, 128, ND, 1024], F32, kind="ExternalInput").ap()
    wk = nc.dram_tensor("wk", [2, 128, ND, 1024], F32, kind="ExternalInput").ap()
    wv = nc.dram_tensor("wv", [2, 128, ND, 1024], F32, kind="ExternalInput").ap()
    wqa = nc.dram_tensor("wqa", [2, 128, ND, 1024], F32, kind="ExternalInput").ap()
    wka = nc.dram_tensor("wka", [2, 128, ND, 1024], F32, kind="ExternalInput").ap()
    gw = nc.dram_tensor("gw", [NI, 128, D], U8, kind="ExternalInput").ap()
    uw = nc.dram_tensor("uw", [NI, 128, D], U8, kind="ExternalInput").ap()
    dw = nc.dram_tensor("dw", [8192, D], U8, kind="ExternalInput").ap()
    par_in = {}
    for nm in ["pgs", "pus"]:
        par_in[nm] = nc.dram_tensor(nm, [128, NI], F32, kind="ExternalInput").ap()
    for nm in ["pds", "pbq", "pbk", "pbv", "pba", "pbb", "plg", "plb"]:
        par_in[nm] = nc.dram_tensor(nm, [128, ND], F32, kind="ExternalInput").ap()
    mask_d = nc.dram_tensor("maskc", [128, 1], F32, kind="ExternalInput").ap()
    slot_d = nc.dram_tensor("slot", [1, 1], I32, kind="ExternalInput").ap()
    nonce_d = nc.dram_tensor("nonce", [1, 1], I32, kind="ExternalInput").ap()

    out = nc.dram_tensor("out", [D, TC], F32, kind="ExternalOutput").ap()
    if dbg in ("ln", "a1", "a1ln", "a1q", "a1kv", "x1", "a2a"):
        dbg_bf = nc.dram_tensor("dbg_bf", [D, TC], BF16,
                                kind="ExternalOutput").ap()
        dbg_row = nc.dram_tensor("dbg_row", [8, 512], F32,
                                 kind="ExternalOutput").ap()
        dbg_car = nc.dram_tensor("dbg_car", [128, 4 * ND], F32,
                                 kind="ExternalOutput").ap()

    x1sp = nc.dram_tensor("x1sp", [D, TC], F32).ap()
    x2sp = nc.dram_tensor("x2sp", [D, TC], F32).ap()
    qsp = nc.dram_tensor("qsp", [D, TC], BF16).ap()
    ksp = nc.dram_tensor("ksp", [D, TC], BF16).ap()
    vsp = nc.dram_tensor("vsp", [D, TC], BF16).ap()
    qwsp = nc.dram_tensor("qwsp", [D, TC], BF16).ap()
    mxsp = nc.dram_tensor("mxsp", [D, TC], BF16).ap()
    kwsp = nc.dram_tensor("kwsp", [D, TC], BF16).ap()
    qwcsp = nc.dram_tensor("qwcsp", [D, TC], F32).ap()
    kwcsp = nc.dram_tensor("kwcsp", [D, TC], F32).ap()
    shc = nc.dram_tensor("shc", [2, 4, 128, ND], F32, addr_space="Shared")
    shf = nc.dram_tensor("shf", [2, 2], I32, addr_space="Shared")
    prog = nc.dram_tensor("prog", [2, 64], I32, addr_space="Shared")

    with ExitStack() as es:
        e = es.enter_context

        sem = {}
        cnt = {}
        for name in ["ly", "tm", "vm", "sm", "gm", "sv"]:
            sem[name] = e(nc.semaphore(name))
            cnt[name] = 0
        lg_pool = [e(nc.semaphore(f"lgx{i}")) for i in range(4)]
        lg_state = {"i": 0, "n": 0}

        def inc(instr, name, k=1):
            instr.then_inc(sem[name], k)
            cnt[name] += k
            return cnt[name]

        def w(eng, name, val=None):
            v = cnt[name] if val is None else val
            if v > 0:
                eng.wait_ge(sem[name], v)

        def g_dma(g, out_ap, in_ap):
            g.dma_start(out=out_ap, in_=in_ap).then_inc(
                lg_pool[lg_state["i"]], 16)
            lg_state["n"] += 16
            return lg_state["n"]

        def g_wait(eng, val=None):
            v = lg_state["n"] if val is None else val
            if v > 0:
                eng.wait_ge(lg_pool[lg_state["i"]], v)

        def g_next_sem():
            lg_state["i"] += 1
            lg_state["n"] = 0

        def vscan(v, out_ap, data_ap, init_ap):
            ins = v.tensor_tensor_scan(out_ap, onesg[:], data_ap,
                                       init_ap, ALU.mult, ALU.add)
            v.drain()
            return ins

        def y_dma(y, out_ap, in_ap):
            return inc(y.dma_start(out=out_ap, in_=in_ap), "ly", 16)

        def fm_load(y, sbuf_t, dram_ap, t0, width):
            src = dram_ap[:, t0:t0 + width].rearrange("(m p) t -> p m t", p=128)
            y_dma(y, sbuf_t[:, 0:8, :], src[:, 0:8, :])
            return y_dma(y, sbuf_t[:, 8:16, :], src[:, 8:16, :])

        def fm_store(y, dram_ap, sbuf_t, t0, width):
            dst = dram_ap[:, t0:t0 + width].rearrange("(m p) t -> p m t", p=128)
            y_dma(y, dst[:, 0:8, :], sbuf_t[:, 0:8, :])
            return y_dma(y, dst[:, 8:16, :], sbuf_t[:, 8:16, :])

        ones_k = e(nc.sbuf_tensor("ones_k", [128, 1], BF16))
        onesbc = e(nc.sbuf_tensor("onesbc", [1, 128], BF16))
        neg128 = e(nc.sbuf_tensor("neg128", [1, 128], BF16))
        onesg = e(nc.sbuf_tensor("onesg", [128, GA], BF16))
        par = {nm: e(nc.sbuf_tensor(nm + "_s", list(par_in[nm].shape), F32))
               for nm in par_in}
        maskc = e(nc.sbuf_tensor("maskc_s", [128, 1], F32))
        slot_sb = e(nc.sbuf_tensor("slot_sb", [1, 1], I32))
        nonce_sb = e(nc.sbuf_tensor("nonce_sb", [1, 1], I32))
        flagv = e(nc.sbuf_tensor("flagv", [1, 2], I32))
        flrd = e(nc.sbuf_tensor("flrd", [1, 1], I32))
        qwc_cr = e(nc.sbuf_tensor("qwc_cr", [128, ND], F32))
        pq_cr = e(nc.sbuf_tensor("pq_cr", [128, ND], F32))
        kwc_cr = e(nc.sbuf_tensor("kwc_cr", [128, ND], F32))
        pk_cr = e(nc.sbuf_tensor("pk_cr", [128, ND], F32))
        spq_pg = [e(nc.sbuf_tensor(f"spq{j}", [128, ND], F32)) for j in range(2)]
        spk_pg = [e(nc.sbuf_tensor(f"spk{j}", [128, ND], F32)) for j in range(2)]
        c1e = e(nc.sbuf_tensor("c1e", [128, ND], F32))
        c3e = e(nc.sbuf_tensor("c3e", [128, ND], F32))
        ptmp = e(nc.sbuf_tensor("ptmp", [128, ND], F32))
        ptmp2 = e(nc.sbuf_tensor("ptmp2", [128, ND], F32))
        r_mu = e(nc.sbuf_tensor("r_mu", [1, 512], F32))
        r_e2 = e(nc.sbuf_tensor("r_e2", [1, 512], F32))
        r_var = e(nc.sbuf_tensor("r_var", [1, 512], F32))
        r_std = e(nc.sbuf_tensor("r_std", [1, 512], F32))
        r_rstd = e(nc.sbuf_tensor("r_rstd", [1, 512], F32))
        r_mu_bf = e(nc.sbuf_tensor("r_mu_bf", [1, 512], BF16))
        r_rstd_bf = e(nc.sbuf_tensor("r_rstd_bf", [1, 512], BF16))
        sxrow = e(nc.sbuf_tensor("sxrow", [1, 512], BF16))
        eps_row = e(nc.sbuf_tensor("eps_row", [1, 1], F32))
        shrow = e(nc.sbuf_tensor("shrow", [1, 512], BF16))

        P = [e(nc.psum_tensor(f"pbk{i}", [128, 512], F32)) for i in range(8)]

        blk = e(nc.Block())

        @blk.sync
        def _(y):
            for nm in par_in:
                y_dma(y, par[nm][:], par_in[nm])
            y_dma(y, maskc[:], mask_d)
            y_dma(y, slot_sb[:], slot_d)
            y_dma(y, nonce_sb[:], nonce_d)

        @blk.vector
        def _(v):
            w(v, "ly")
            v.tensor_scalar_mul(par["pds"][:], par["pds"][:], 0.5)
            v.tensor_scalar_mul(par["pba"][:], par["pba"][:], SCALE)
            v.tensor_scalar_mul(par["pbb"][:], par["pbb"][:], SCALE)
            v.memset(qwc_cr[:], 0.0)
            v.memset(kwc_cr[:], 0.0)
            v.memset(pq_cr[:], 0.0)
            v.memset(pk_cr[:], 0.0)
            v.memset(ones_k[:], 1.0)
            v.memset(onesbc[:], 1.0)
            v.memset(neg128[:], -128.0)
            v.memset(onesg[:], 1.0)
            v.memset(eps_row[:], EPS)
            v.tensor_scalar_add(flagv[:, 0:1], nonce_sb[:], 1)
            inc(v.tensor_scalar_add(flagv[:, 1:2], nonce_sb[:], 2), "vm")

        pmark_sb = e(nc.sbuf_tensor("pmark_sb", [1, 64], I32))
        pm_state = {"n": 0}

        @blk.vector
        def _(v):
            for j in range(64):
                v.memset(pmark_sb[:, j:j + 1], 1000 + j)
            inc(v.nop(), "vm")

        def pmark():
            j = pm_state["n"]
            pm_state["n"] += 1

            @blk.gpsimd
            def _(g):
                w(g, "tm")
                w(g, "vm")
                w(g, "sm")
                w(g, "ly")
                g_dma(g, prog.ap()[0:1, j:j + 1], pmark_sb[:, j:j + 1])

        def ln_pass(x_view, width, apply_fn, xbf, sqf):
            for sl in range(width // 512):
                a = sl * 512

                sm0 = cnt["sm"]
                tm0 = cnt["tm"]

                @blk.scalar
                def _(s, a=a, sm0=sm0, tm0=tm0):
                    w(s, "vm")
                    w(s, "tm")
                    for m in range(ND):
                        if m >= 2:
                            w(s, "tm", tm0 + m - 1)
                        s.copy(xbf[m % 2][:], x_view(m)[:, a:a + 512])
                        inc(s.square(sqf[m % 2][:], x_view(m)[:, a:a + 512]),
                            "sm")
                assert cnt["sm"] == sm0 + ND

                @blk.tensor
                def _(t, sm0=sm0):
                    w(t, "vm")
                    for m in range(ND):
                        w(t, "sm", sm0 + m + 1)
                        t.matmul(P[0][0:1, :], ones_k[:], xbf[m % 2][:],
                                 start=(m == 0), stop=(m == ND - 1))
                        inc(t.matmul(P[1][0:1, :], ones_k[:], sqf[m % 2][:],
                                     start=(m == 0), stop=(m == ND - 1)), "tm")
                assert cnt["tm"] == tm0 + ND
                stat_tm = cnt["tm"]

                @blk.vector
                def _(v):
                    w(v, "tm", stat_tm)
                    v.tensor_scalar_mul(r_mu[:], P[0][0:1, :], INV_D)
                    v.tensor_scalar_mul(r_e2[:], P[1][0:1, :], INV_D)
                    v.tensor_mul(r_var[:], r_mu[:], r_mu[:])
                    v.tensor_sub(r_var[:], r_e2[:], r_var[:])
                    inc(v.tensor_copy(r_mu_bf[:], r_mu[:]), "vm")
                var_vm = cnt["vm"]

                @blk.scalar
                def _(s):
                    w(s, "vm", var_vm)
                    inc(s.activation(r_std[:], r_var[:], AF.Sqrt,
                                     bias=eps_row[:], scale=1.0), "sm")
                std_sm = cnt["sm"]

                @blk.vector
                def _(v):
                    w(v, "sm", std_sm)
                    v.reciprocal(r_rstd[:], r_std[:])
                    inc(v.tensor_copy(r_rstd_bf[:], r_rstd[:]), "vm")
                row_vm = cnt["vm"]

                @blk.tensor
                def _(t):
                    w(t, "vm", row_vm)
                    t.matmul(P[2][:], onesbc[:], r_mu_bf[:],
                             start=True, stop=True)
                    inc(t.matmul(P[3][:], onesbc[:], r_rstd_bf[:],
                                 start=True, stop=True), "tm")
                bc_tm = cnt["tm"]

                @blk.vector
                def _(v, a=a):
                    w(v, "tm", bc_tm)
                    for m in range(ND):
                        last = apply_fn(v, m, P[2][:], P[3][:], a, a + 512)
                    inc(last, "vm")

        from contextlib import contextmanager

        @contextmanager
        def my_fori(eng, start, endv):
            lid = nc.next_id()
            ls, le_ = f"mf{lid}_loop", f"mf{lid}_end"
            regs = []
            with eng.register(f"mf{lid}_i") as r:
                eng.reg_mov(r, start)
                eng.br(ls)
                with nc.body(ls, valid_engines=[eng.engine]):
                    sv = eng.snap(r, min_val=start, max_val=endv - 1)
                    yield sv, regs
                    for h in set(regs):
                        try:
                            eng.free_register(h)
                        except Exception:
                            pass
                    eng.reg_add(r, r, 1)
                    eng.br_lt(r, endv, ls, le_)
                nc.switch_bb(le_)

        def ffn_phase(src_dram, dst_dram, tag, upto="full"):
            for gi in range(NGF):
                t0 = gi * GF
                with ExitStack() as ps:
                    pe = ps.enter_context
                    x_grp = pe(nc.sbuf_tensor(f"x_{tag}{gi}", [128, ND, GF], F32))
                    xln = pe(nc.sbuf_tensor(f"xl_{tag}{gi}", [128, ND, GF], BF16))
                    hmid = pe(nc.sbuf_tensor(f"hm_{tag}{gi}", [128, NI, GF], BF16))
                    xbf = [pe(nc.sbuf_tensor(f"xb{j}_{tag}{gi}", [128, 512],
                                             BF16)) for j in range(2)]
                    sqf = [pe(nc.sbuf_tensor(f"sq{j}_{tag}{gi}", [128, 512],
                                             BF16)) for j in range(2)]
                    lnt = pe(nc.sbuf_tensor(f"lt_{tag}{gi}", [128, 512], F32))
                    silb = [pe(nc.sbuf_tensor(f"si{j}_{tag}{gi}", [128, GF],
                                              F32)) for j in range(2)]

                    @blk.sync
                    def _(y):
                        w(y, "vm")
                        fm_load(y, x_grp, src_dram, t0, GF)
                    xld = cnt["ly"]

                    @blk.scalar
                    def _(s):
                        w(s, "ly", xld)
                        inc(s.nop(), "sm")

                    def apply_f(v, m, mups, rps, a, b):
                        v.tensor_sub(lnt[:], x_grp[:, m, a:b], mups)
                        return v.tensor_mul(xln[:, m, a:b], lnt[:], rps)

                    if upto == "x0":
                        @blk.sync
                        def _(y):
                            w(y, "ly")
                            fm_store(y, dst_dram, x_grp, t0, GF)

                        @blk.vector
                        def _(v):
                            w(v, "ly")
                            inc(v.nop(), "vm")
                        continue

                    ln_pass(lambda m: x_grp[:, m, :], GF, apply_f, xbf, sqf)
                    xln_vm = cnt["vm"]
                    if upto == "ln":
                        @blk.sync
                        def _(y, gi=gi):
                            w(y, "vm", xln_vm)
                            fm_store(y, dbg_bf, xln, t0, GF)
                            if gi == 0:
                                for j, rt in enumerate(
                                        [r_mu, r_e2, r_var, r_std, r_rstd]):
                                    y_dma(y, dbg_row[j:j + 1, :], rt[:])

                        @blk.vector
                        def _(v):
                            w(v, "ly")
                            inc(v.nop(), "vm")
                        continue

                    @blk.tensor
                    def _(t):
                        w(t, "vm", xln_vm)
                        for m in range(ND):
                            ins = t.matmul(P[0][0:1, :], ones_k[:],
                                           xln[:, m, :],
                                           start=(m == 0), stop=(m == ND - 1))
                        inc(ins, "tm")
                    sx_tm = cnt["tm"]

                    @blk.vector
                    def _(v):
                        w(v, "tm", sx_tm)
                        inc(v.tensor_copy(sxrow[:], P[0][0:1, :]), "vm")
                    sx_vm = cnt["vm"]

                    with ExitStack() as ws:
                        we = ws.enter_context
                        gwb = [we(nc.sbuf_tensor(f"gw{j}_{tag}{gi}",
                                                 [128, WTILE, D], BF16))
                               for j in range(2)]
                        uwb = [we(nc.sbuf_tensor(f"uw{j}_{tag}{gi}",
                                                 [128, WTILE, D], BF16))
                               for j in range(2)]
                        lg0 = lg_state["n"]
                        tm0 = cnt["tm"]
                        vm0 = cnt["vm"]
                        sm0 = cnt["sm"]
                        W2 = 2 * WTILE

                        @blk.gpsimd
                        def _(g):
                            w(g, "tm", tm0)
                            w(g, "vm", vm0)
                            for ti in range(NTI):
                                if ti >= 2:
                                    w(g, "tm", tm0 + (ti - 1) * WTILE)
                                g_dma(g, gwb[ti % 2][:],
                                      gw[ti * WTILE:(ti + 1) * WTILE]
                                      .rearrange("c p h -> p c h"))
                                g_dma(g, uwb[ti % 2][:],
                                      uw[ti * WTILE:(ti + 1) * WTILE]
                                      .rearrange("c p h -> p c h"))
                        assert lg_state["n"] == lg0 + 32 * NTI

                        def gu_T(t, jj, rlg=None, rvm=None):
                            for tp in range(2):
                                if rlg is not None:
                                    t.wait_ge(lg_pool[lg_state["i"]], rlg)
                                    t.reg_add(rlg, rlg, 32)
                                else:
                                    t.wait_ge(lg_pool[lg_state["i"]],
                                              lg0 + (jj * 2 + tp) * 32 + 32)
                                for c in range(WTILE):
                                    icb = WTILE * tp + c
                                    bg = P[4 + (icb % 2) * 2]
                                    bu = P[5 + (icb % 2) * 2]
                                    if rvm is not None:
                                        t.wait_ge(sem["vm"], rvm)
                                        t.reg_add(rvm, rvm, 1)
                                    elif not (jj == 0 and icb < 2):
                                        t.wait_ge(sem["vm"],
                                                  vm0 + jj * W2 + icb - 1)
                                    for k in range(ND):
                                        t.matmul(bg[:],
                                                 gwb[tp][:, c,
                                                         k * 128:(k + 1) * 128],
                                                 xln[:, k, :],
                                                 start=(k == 0), stop=False)
                                    t.matmul(bg[:], neg128[:], sxrow[:],
                                             start=False, stop=True)
                                    for k in range(ND):
                                        t.matmul(bu[:],
                                                 uwb[tp][:, c,
                                                         k * 128:(k + 1) * 128],
                                                 xln[:, k, :],
                                                 start=(k == 0), stop=False)
                                    t.matmul(bu[:], neg128[:], sxrow[:],
                                             start=False,
                                             stop=True).then_inc(sem["tm"], 1)

                        @blk.tensor
                        def _(t):
                            w(t, "vm", sx_vm)
                            gu_T(t, 0)
                            with t.register(f"rlg_{tag}{gi}") as rlg, \
                                    t.register(f"rvm_{tag}{gi}") as rvm:
                                t.reg_mov(rlg, lg0 + 96)
                                t.reg_mov(rvm, vm0 + W2 - 1)
                                with my_fori(t, 1, NTI // 2) as (j, regs):
                                    gu_T(t, j, rlg, rvm)
                        cnt["tm"] += NI

                        @blk.scalar
                        def _(s):
                            for ic in range(NI):
                                w(s, "tm", tm0 + ic + 1)
                                if ic >= 2:
                                    w(s, "vm", vm0 + ic - 1)
                                inc(s.activation(silb[ic % 2][:],
                                                 P[4 + (ic % 2) * 2][:],
                                                 AF.Silu, bias=0.0,
                                                 scale=par["pgs"][:, ic:ic + 1]),
                                    "sm")
                        assert cnt["sm"] == sm0 + NI

                        @blk.vector
                        def _(v):
                            for ic in range(NI):
                                w(v, "sm", sm0 + ic + 1)
                                inc(v.scalar_tensor_tensor(
                                    hmid[:, ic, :], P[5 + (ic % 2) * 2][:],
                                    par["pus"][:, ic:ic + 1], silb[ic % 2][:],
                                    ALU.mult, ALU.mult), "vm")
                        assert cnt["vm"] == vm0 + NI
                    hmid_vm = cnt["vm"]

                    @blk.tensor
                    def _(t):
                        w(t, "vm", hmid_vm)
                        for ic in range(NI):
                            ins = t.matmul(P[0][0:1, :], ones_k[:],
                                           hmid[:, ic, :],
                                           start=(ic == 0), stop=(ic == NI - 1))
                        inc(ins, "tm")
                    sh_tm = cnt["tm"]

                    @blk.vector
                    def _(v):
                        w(v, "tm", sh_tm)
                        inc(v.tensor_copy(shrow[:], P[0][0:1, :]), "vm")
                    sh_vm = cnt["vm"]

                    for mh in range(2):
                        with ExitStack() as ws:
                            we = ws.enter_context
                            dwb = [we(nc.sbuf_tensor(
                                f"dw{j}_{tag}{gi}{mh}", [128, WTILE, D],
                                BF16)) for j in range(2)]
                            lg0 = lg_state["n"]
                            tm0 = cnt["tm"]

                            @blk.gpsimd
                            def _(g):
                                w(g, "tm", tm0)
                                w(g, "vm", cnt["vm"])
                                for ti in range(NTI):
                                    if ti >= 2:
                                        w(g, "tm", tm0 + ti - 1)
                                    g_dma(g, dwb[ti % 2][:],
                                          dw[ti * WTILE * 128:
                                             (ti + 1) * WTILE * 128, :]
                                          .rearrange("(c p) h -> p c h", p=128))
                            assert lg_state["n"] == lg0 + 16 * NTI

                            @blk.tensor
                            def _(t):
                                w(t, "vm", sh_vm)
                                w(t, "vm")
                                for ti in range(NTI):
                                    g_wait(t, lg0 + 16 * (ti + 1))
                                    for c in range(WTILE):
                                        ic = ti * WTILE + c
                                        for ml in range(8):
                                            m = mh * 8 + ml
                                            ins = t.matmul(
                                                P[ml][:],
                                                dwb[ti % 2][:, c,
                                                            m * 128:(m + 1) * 128],
                                                hmid[:, ic, :],
                                                start=(ic == 0), stop=False)
                                    inc(ins, "tm")
                                for ml in range(8):
                                    ins = t.matmul(P[ml][:], neg128[:],
                                                   shrow[:],
                                                   start=False, stop=True)
                                inc(ins, "tm")
                            assert cnt["tm"] == tm0 + NTI + 1
                            dn_tm = cnt["tm"]

                            @blk.vector
                            def _(v, mh=mh):
                                w(v, "tm", dn_tm)
                                for ml in range(8):
                                    m = mh * 8 + ml
                                    ins = v.scalar_tensor_tensor(
                                        x_grp[:, m, :], P[ml][:],
                                        par["pds"][:, m:m + 1],
                                        x_grp[:, m, :], ALU.mult, ALU.add)
                                inc(ins, "vm")

                    x_done = cnt["vm"]

                    @blk.sync
                    def _(y):
                        w(y, "vm", x_done)
                        fm_store(y, dst_dram, x_grp, t0, GF)

                    @blk.vector
                    def _(v):
                        w(v, "ly")
                        inc(v.nop(), "vm")

        def a1_phase(upto="full"):

            def do_matrix(whalves, moving, wmb):
                lg0 = lg_state["n"]
                tmb = cnt["tm"]
                vmb = cnt["vm"]

                @blk.gpsimd
                def _(g):
                    w(g, "tm", tmb)
                    w(g, "vm", vmb)
                    for mh in range(2):
                        g_dma(g, wmb[mh][:, 0:8, :], whalves[mh][:, 0:8, :])
                        g_dma(g, wmb[mh][:, 8:16, :], whalves[mh][:, 8:16, :])

                bs = {}
                smb = cnt["sm"]

                @blk.tensor
                def _(t):
                    w(t, "vm", vmb)
                    w(t, "sm", smb)
                    for mh in range(2):
                        g_wait(t, lg0 + 32 * (mh + 1))
                        for ml in range(8):
                            m = mh * 8 + ml
                            for sl in range(2):
                                if m >= 2:
                                    w(t, "sm", smb + 2 * (m - 2) + sl + 1)
                                for k in range(ND):
                                    ins = t.matmul(
                                        P[4 + (m % 2) * 2 + sl][:],
                                        wmb[mh][:, k, ml * 128:(ml + 1) * 128],
                                        moving[:, k, sl * 512:(sl + 1) * 512],
                                        start=(k == 0), stop=(k == ND - 1))
                                bs[(m, sl)] = inc(ins, "tm")
                return bs, smb

            def bank(m, sl):
                return P[4 + (m % 2) * 2 + sl]

            for gi in range(NGA):
                t0 = gi * GA
                with ExitStack() as ps:
                    pe = ps.enter_context
                    h2 = pe(nc.sbuf_tensor(f"h2_{gi}", [128, ND, GA], BF16))
                    with ExitStack() as lns:
                        le = lns.enter_context
                        x1h = le(nc.sbuf_tensor(f"x1h_{gi}", [128, ND, GA],
                                                F32))
                        xbf = [le(nc.sbuf_tensor(f"ab{j}_{gi}", [128, 512],
                                                 BF16)) for j in range(2)]
                        sqf = [le(nc.sbuf_tensor(f"as{j}_{gi}", [128, 512],
                                                 BF16)) for j in range(2)]
                        lnt = le(nc.sbuf_tensor(f"al_{gi}", [128, 512], F32))

                        @blk.sync
                        def _(y):
                            w(y, "vm")
                            fm_load(y, x1h, x1sp, t0, GA)
                        x1ld = cnt["ly"]

                        @blk.scalar
                        def _(s):
                            w(s, "ly", x1ld)
                            inc(s.nop(), "sm")

                        def apply1(v, m, mups, rps, a, b):
                            v.tensor_sub(x1h[:, m, a:b], x1h[:, m, a:b], mups)
                            return v.tensor_mul(x1h[:, m, a:b],
                                                x1h[:, m, a:b], rps)

                        ln_pass(lambda m: x1h[:, m, :], GA, apply1, xbf, sqf)

                        def apply2(v, m, mups, rps, a, b):
                            v.tensor_sub(lnt[:], x1h[:, m, a:b], mups)
                            v.tensor_mul(lnt[:], lnt[:], rps)
                            return v.tensor_scalar(h2[:, m, a:b], lnt[:],
                                                   par["plg"][:, m:m + 1],
                                                   par["plb"][:, m:m + 1],
                                                   ALU.mult, ALU.add)

                        ln_pass(lambda m: x1h[:, m, :], GA, apply2, xbf, sqf)
                    h2_vm = cnt["vm"]
                    if upto == "ln":
                        @blk.sync
                        def _(y):
                            w(y, "vm", h2_vm)
                            fm_store(y, qsp, h2, t0, GA)

                        @blk.vector
                        def _(v):
                            w(v, "ly")
                            inc(v.nop(), "vm")
                        continue

                    wmb = [pe(nc.sbuf_tensor(f"wm{j}_{gi}", [128, ND, 1024],
                                             BF16)) for j in range(2)]
                    q_res = pe(nc.sbuf_tensor(f"qr_{gi}", [128, ND, GA], BF16))
                    kvb = [pe(nc.sbuf_tensor(f"kv{j}_{gi}", [128, GA], BF16))
                           for j in range(2)]
                    qwb = [pe(nc.sbuf_tensor(f"qw{j}_{gi}", [128, GA], BF16))
                           for j in range(2)]
                    qcb = [pe(nc.sbuf_tensor(f"qc{j}_{gi}", [128, GA], F32))
                           for j in range(2)]
                    scr1 = pe(nc.sbuf_tensor(f"s1_{gi}", [128, GA], F32))
                    scr2 = pe(nc.sbuf_tensor(f"s2_{gi}", [128, GA], F32))

                    bs, smb_q = do_matrix([wq[0], wq[1]], h2, wmb)

                    assert cnt["sm"] == smb_q

                    @blk.scalar
                    def _(s):
                        w(s, "ly")
                        for m in range(ND):
                            for sl in range(2):
                                w(s, "tm", bs[(m, sl)])
                                inc(s.activation(
                                    q_res[:, m, sl * 512:(sl + 1) * 512],
                                    bank(m, sl)[:], AF.Identity,
                                    bias=par["pbq"][:, m:m + 1], scale=1.0),
                                    "sm")
                    assert cnt["sm"] == smb_q + 2 * ND

                    @blk.sync
                    def _(y):
                        w(y, "sm")
                        fm_store(y, qsp, q_res, t0, GA)

                    if upto == "q":
                        @blk.vector
                        def _(v):
                            w(v, "ly")
                            inc(v.nop(), "vm")
                        continue

                    for wd_, sp_, pbn in [(wk, ksp, "pbk"), (wv, vsp, "pbv")]:
                        bs, smb_kv = do_matrix([wd_[0], wd_[1]], h2, wmb)
                        ly0 = cnt["ly"]
                        sm0 = cnt["sm"]

                        assert cnt["sm"] == smb_kv

                        @blk.scalar
                        def _(s, bs=bs, pbn=pbn, ly0=ly0):
                            for m in range(ND):
                                for sl in range(2):
                                    w(s, "tm", bs[(m, sl)])
                                    w(s, "ly", ly0 if m < 2
                                      else ly0 + 16 * (m - 1))
                                    inc(s.activation(
                                        kvb[m % 2][:,
                                                   sl * 512:(sl + 1) * 512],
                                        bank(m, sl)[:], AF.Identity,
                                        bias=par[pbn][:, m:m + 1], scale=1.0),
                                        "sm")
                        assert cnt["sm"] == smb_kv + 2 * ND

                        @blk.sync
                        def _(y, sp_=sp_, sm0=sm0):
                            for m in range(ND):
                                w(y, "sm", sm0 + 2 * (m + 1))
                                y_dma(y, sp_[m * 128:(m + 1) * 128,
                                             t0:t0 + GA], kvb[m % 2][:])
                        assert cnt["ly"] == ly0 + 16 * ND

                    if upto == "kv":
                        @blk.vector
                        def _(v):
                            w(v, "ly")
                            inc(v.nop(), "vm")
                        continue

                    bs, smb_qa = do_matrix([wqa[0], wqa[1]], q_res, wmb)
                    sm0 = cnt["sm"]
                    vm0 = cnt["vm"]
                    ly0 = cnt["ly"]

                    assert sm0 == smb_qa

                    @blk.scalar
                    def _(s):
                        for m in range(ND):
                            for sl in range(2):
                                w(s, "tm", bs[(m, sl)])
                                if m >= 2:
                                    w(s, "vm", vm0 + m - 1)
                                    w(s, "ly", ly0 + 32 * (m - 1))
                                inc(s.activation(
                                    qwb[m % 2][:, sl * 512:(sl + 1) * 512],
                                    bank(m, sl)[:], AF.Exp,
                                    bias=par["pba"][:, m:m + 1], scale=SCALE),
                                    "sm")
                    assert cnt["sm"] == sm0 + 2 * ND

                    @blk.vector
                    def _(v):
                        for m in range(ND):
                            w(v, "sm", sm0 + 2 * (m + 1))
                            if m >= 2:
                                w(v, "ly", ly0 + 32 * (m - 1))
                            vscan(v, qcb[m % 2][:], qwb[m % 2][:],
                                  qwc_cr[:, m:m + 1])
                            v.tensor_copy(qwc_cr[:, m:m + 1],
                                          qcb[m % 2][:, GA - 1:GA])
                            v.reciprocal(scr1[:], qcb[m % 2][:])
                            v.tensor_mul(scr2[:], qwb[m % 2][:], scr1[:])
                            v.scalar_tensor_tensor(
                                scr1[:], scr2[:], 1.0, q_res[:, m, :],
                                ALU.mult, ALU.mult,
                                accum_out=spq_pg[gi][:, m:m + 1])
                            v.drain()
                            inc(v.nop(), "vm")
                    assert cnt["vm"] == vm0 + ND

                    @blk.sync
                    def _(y):
                        for m in range(ND):
                            w(y, "vm", vm0 + m + 1)
                            y_dma(y, qwsp[m * 128:(m + 1) * 128, t0:t0 + GA],
                                  qwb[m % 2][:])
                            y_dma(y, qwcsp[m * 128:(m + 1) * 128, t0:t0 + GA],
                                  qcb[m % 2][:])
                    assert cnt["ly"] == ly0 + 32 * ND

                    @blk.vector
                    def _(v):
                        w(v, "ly")
                        inc(v.nop(), "vm")

        def x_exchange(round_i):
            src_pg = spq_pg if round_i == 0 else spk_pg
            cr_in = qwc_cr if round_i == 0 else kwc_cr
            c_eff = c1e if round_i == 0 else c3e
            p_init = pq_cr if round_i == 0 else pk_cr

            @blk.vector
            def _(v):
                inc(v.tensor_add(ptmp2[:], src_pg[0][:], src_pg[1][:]), "vm")
            vm_ready = cnt["vm"]

            @blk.gpsimd
            def _(g):
                w(g, "vm", vm_ready)
                s = g.value_load(slot_sb[0:1, 0:1])
                p = g.compute_val(1 - s)
                g_dma(g, shc.ap()[bass.ts(s, 1), 0 + 2 * round_i:
                                  1 + 2 * round_i, :, :], cr_in[:])
                g_dma(g, shc.ap()[bass.ts(s, 1), 1 + 2 * round_i:
                                  2 + 2 * round_i, :, :], ptmp2[:])
                g_wait(g)
                g_dma(g, shf.ap()[bass.ts(s, 1),
                                  round_i:round_i + 1],
                      flagv[:, round_i:round_i + 1])
                g_wait(g)
                tgt = g.compute_val(g.value_load(nonce_sb[0:1, 0:1])
                                    + (round_i + 1))
                with g.register(f"pt{round_i}") as t:
                    g.reg_mov(t, lg_state["n"])

                    def cond():
                        g.dma_start(out=flrd[:],
                                    in_=shf.ap()[bass.ts(p, 1),
                                                 round_i:round_i + 1]
                                    ).then_inc(lg_pool[lg_state["i"]], 16)
                        g.reg_add(t, t, 16)
                        g.wait_ge(lg_pool[lg_state["i"]], t)
                        f = g.value_load(flrd[0:1, 0:1])
                        return g.compute_val(f != tgt)

                    with g.While(cond):
                        g.nop()
                    g.dma_start(out=ptmp[:],
                                in_=shc.ap()[bass.ts(p, 1),
                                             0 + 2 * round_i:1 + 2 * round_i,
                                             :, :]
                                ).then_inc(lg_pool[lg_state["i"]], 16)
                    g.reg_add(t, t, 16)
                    g.dma_start(out=ptmp2[:],
                                in_=shc.ap()[bass.ts(p, 1),
                                             1 + 2 * round_i:2 + 2 * round_i,
                                             :, :]
                                ).then_inc(lg_pool[lg_state["i"]], 16)
                    g.reg_add(t, t, 16)
                    g.wait_ge(lg_pool[lg_state["i"]], t)
                g.tensor_scalar(p_init[:], ptmp2[:], maskc[:, 0:1], None,
                                ALU.mult)
                inc(g.tensor_scalar(c_eff[:], ptmp[:], maskc[:, 0:1], None,
                                    ALU.mult), "gm")
            g_next_sem()

        def a2a_phase():
            for gi in range(NGA):
                t0 = gi * GA
                with ExitStack() as ps:
                    pe = ps.enter_context
                    mixed = pe(nc.sbuf_tensor(f"mx_{gi}", [128, ND, GA], BF16))
                    wmb = [pe(nc.sbuf_tensor(f"km{j}_{gi}", [128, ND, 1024],
                                             BF16)) for j in range(2)]
                    qb = [pe(nc.sbuf_tensor(f"aq{j}_{gi}", [128, GA], BF16))
                          for j in range(2)]
                    qwb = [pe(nc.sbuf_tensor(f"aw{j}_{gi}", [128, GA], BF16))
                           for j in range(2)]
                    kb = [pe(nc.sbuf_tensor(f"ak{j}_{gi}", [128, GA], BF16))
                          for j in range(2)]
                    qcb = [pe(nc.sbuf_tensor(f"ac{j}_{gi}", [128, GA], F32))
                           for j in range(2)]
                    kwb = [pe(nc.sbuf_tensor(f"bw{j}_{gi}", [128, GA], BF16))
                           for j in range(2)]
                    kcb = [pe(nc.sbuf_tensor(f"bc{j}_{gi}", [128, GA], F32))
                           for j in range(2)]
                    scr1 = pe(nc.sbuf_tensor(f"c1_{gi}", [128, GA], F32))
                    scr2 = pe(nc.sbuf_tensor(f"c2_{gi}", [128, GA], F32))
                    scr3 = pe(nc.sbuf_tensor(f"c3_{gi}", [128, GA], F32))

                    ld = {}
                    vm0 = cnt["vm"] + 1

                    @blk.sync
                    def _(y):
                        w(y, "vm", cnt["vm"])
                        for m in range(ND):
                            if m >= 2:
                                w(y, "vm", vm0 + m - 1)
                            r0 = m * 128
                            y_dma(y, qb[m % 2][:], qsp[r0:r0 + 128, t0:t0 + GA])
                            y_dma(y, qwb[m % 2][:],
                                  qwsp[r0:r0 + 128, t0:t0 + GA])
                            y_dma(y, qcb[m % 2][:],
                                  qwcsp[r0:r0 + 128, t0:t0 + GA])
                            ld[m] = y_dma(y, kb[m % 2][:],
                                          ksp[r0:r0 + 128, t0:t0 + GA])

                    @blk.vector
                    def _(v):
                        w(v, "gm")
                        inc(v.nop(), "vm")
                        assert cnt["vm"] == vm0
                        for m in range(ND):
                            w(v, "ly", ld[m])
                            v.tensor_scalar(scr1[:], qcb[m % 2][:],
                                            c1e[:, m:m + 1], None, ALU.add)
                            v.reciprocal(scr2[:], scr1[:])
                            v.tensor_mul(scr1[:], qwb[m % 2][:], scr2[:])
                            v.tensor_mul(scr2[:], scr1[:], qb[m % 2][:])
                            vscan(v, scr3[:], scr2[:],
                                  pq_cr[:, m:m + 1])
                            v.tensor_copy(pq_cr[:, m:m + 1],
                                          scr3[:, GA - 1:GA])
                            inc(v.tensor_mul(mixed[:, m, :], scr3[:],
                                             kb[m % 2][:]), "vm")
                    assert cnt["vm"] == vm0 + ND
                    mixed_vm = cnt["vm"]

                    @blk.sync
                    def _(y):
                        w(y, "vm", mixed_vm)
                        fm_store(y, mxsp, mixed, t0, GA)

                    lg0 = lg_state["n"]
                    tmb = cnt["tm"]

                    @blk.gpsimd
                    def _(g):
                        w(g, "vm", mixed_vm)
                        w(g, "tm", tmb)
                        for mh in range(2):
                            g_dma(g, wmb[mh][:, 0:8, :], wka[mh][:, 0:8, :])
                            g_dma(g, wmb[mh][:, 8:16, :], wka[mh][:, 8:16, :])

                    bs = {}
                    smb = cnt["sm"]

                    @blk.tensor
                    def _(t):
                        w(t, "vm", mixed_vm)
                        w(t, "sm", smb)
                        for mh in range(2):
                            g_wait(t, lg0 + 32 * (mh + 1))
                            for ml in range(8):
                                m = mh * 8 + ml
                                for sl in range(2):
                                    if m >= 2:
                                        w(t, "sm", smb + 2 * (m - 2) + sl + 1)
                                    for k in range(ND):
                                        ins = t.matmul(
                                            P[4 + (m % 2) * 2 + sl][:],
                                            wmb[mh][:, k,
                                                    ml * 128:(ml + 1) * 128],
                                            mixed[:, k,
                                                  sl * 512:(sl + 1) * 512],
                                            start=(k == 0), stop=(k == ND - 1))
                                    bs[(m, sl)] = inc(ins, "tm")

                    sm0 = cnt["sm"]
                    vm0 = cnt["vm"]
                    ly0 = cnt["ly"]

                    assert sm0 == smb

                    @blk.scalar
                    def _(s):
                        for m in range(ND):
                            for sl in range(2):
                                w(s, "tm", bs[(m, sl)])
                                if m >= 2:
                                    w(s, "vm", vm0 + m - 1)
                                    w(s, "ly", ly0 + 32 * (m - 1))
                                inc(s.activation(
                                    kwb[m % 2][:, sl * 512:(sl + 1) * 512],
                                    P[4 + (m % 2) * 2 + sl][:], AF.Exp,
                                    bias=par["pbb"][:, m:m + 1], scale=SCALE),
                                    "sm")
                    assert cnt["sm"] == smb + 2 * ND

                    @blk.vector
                    def _(v):
                        for m in range(ND):
                            w(v, "sm", sm0 + 2 * (m + 1))
                            if m >= 2:
                                w(v, "ly", ly0 + 32 * (m - 1))
                            vscan(v, kcb[m % 2][:], kwb[m % 2][:],
                                  kwc_cr[:, m:m + 1])
                            v.tensor_copy(kwc_cr[:, m:m + 1],
                                          kcb[m % 2][:, GA - 1:GA])
                            v.reciprocal(scr1[:], kcb[m % 2][:])
                            v.tensor_mul(scr2[:], kwb[m % 2][:], scr1[:])
                            v.scalar_tensor_tensor(
                                scr1[:], scr2[:], 1.0, mixed[:, m, :],
                                ALU.mult, ALU.mult,
                                accum_out=spk_pg[gi][:, m:m + 1])
                            v.drain()
                            inc(v.nop(), "vm")
                    assert cnt["vm"] == vm0 + ND

                    @blk.sync
                    def _(y):
                        for m in range(ND):
                            w(y, "vm", vm0 + m + 1)
                            y_dma(y, kwsp[m * 128:(m + 1) * 128, t0:t0 + GA],
                                  kwb[m % 2][:])
                            y_dma(y, kwcsp[m * 128:(m + 1) * 128, t0:t0 + GA],
                                  kcb[m % 2][:])
                    assert cnt["ly"] == ly0 + 32 * ND

                    @blk.vector
                    def _(v):
                        w(v, "ly")
                        inc(v.nop(), "vm")

        def a2b_phase(upto="full"):
            for gi in range(NGA):
                t0 = gi * GA
                with ExitStack() as ps:
                    pe = ps.enter_context
                    kwbh = pe(nc.sbuf_tensor(f"zk_{gi}", [128, 8, GA], BF16))
                    kcbh = pe(nc.sbuf_tensor(f"zc_{gi}", [128, 8, GA], F32))
                    mxbh = pe(nc.sbuf_tensor(f"zm_{gi}", [128, 8, GA], BF16))
                    vbh = pe(nc.sbuf_tensor(f"zv_{gi}", [128, 8, GA], BF16))
                    x1bh = pe(nc.sbuf_tensor(f"zx_{gi}", [128, 8, GA], F32))
                    x2bh = pe(nc.sbuf_tensor(f"zy_{gi}", [128, 8, GA], F32))
                    scr1 = pe(nc.sbuf_tensor(f"zs1_{gi}", [128, GA], F32))
                    scr2 = pe(nc.sbuf_tensor(f"zs2_{gi}", [128, GA], F32))
                    scr3 = pe(nc.sbuf_tensor(f"zs3_{gi}", [128, GA], F32))

                    ld = {}
                    stm = {}
                    vm_base = cnt["vm"]
                    vmh = {hf: vm_base + 1 + hf + 1 for hf in range(2)}

                    @blk.sync
                    def _(y):
                        w(y, "vm", vm_base)
                        for hf in range(2):
                            if hf > 0:
                                w(y, "vm", vmh[hf - 1])
                            r0 = hf * 8 * 128
                            rows = slice(r0, r0 + 8 * 128)
                            for dst, sp_ in [(kwbh, kwsp), (kcbh, kwcsp),
                                             (mxbh, mxsp), (vbh, vsp),
                                             (x1bh, x1sp)]:
                                ld[hf] = y_dma(
                                    y, dst[:],
                                    sp_[rows, t0:t0 + GA].rearrange(
                                        "(m p) t -> p m t", p=128))
                            if hf > 0:
                                w(y, "vm", vmh[hf - 1])
                                stm[hf - 1] = y_dma(
                                    y, x2sp[(hf - 1) * 1024:hf * 1024,
                                            t0:t0 + GA].rearrange(
                                        "(m p) t -> p m t", p=128), x2bh[:])

                    @blk.vector
                    def _(v):
                        w(v, "gm")
                        inc(v.nop(), "vm")
                        for hf in range(2):
                            w(v, "ly", ld[hf])
                            if hf > 0:
                                w(v, "ly", stm[hf - 1])
                            for ml in range(8):
                                m = hf * 8 + ml
                                if upto == "loads":
                                    continue
                                v.tensor_scalar(scr1[:], kcbh[:, ml, :],
                                                c3e[:, m:m + 1], None, ALU.add)
                                v.reciprocal(scr2[:], scr1[:])
                                v.tensor_mul(scr1[:], kwbh[:, ml, :], scr2[:])
                                v.tensor_mul(scr2[:], scr1[:], mxbh[:, ml, :])
                                if upto == "noscan":
                                    v.tensor_copy(scr3[:], scr2[:])
                                else:
                                    vscan(v, scr3[:], scr2[:],
                                          pk_cr[:, m:m + 1])
                                    v.tensor_copy(pk_cr[:, m:m + 1],
                                                  scr3[:, GA - 1:GA])
                                v.tensor_mul(scr1[:], scr3[:], vbh[:, ml, :])
                                ins = v.tensor_add(x2bh[:, ml, :], scr1[:],
                                                   x1bh[:, ml, :])
                            if upto == "loads":
                                ins = v.nop()
                            assert cnt["vm"] + 1 == vmh[hf]
                            inc(ins, "vm")

                    @blk.sync
                    def _(y):
                        w(y, "vm", vmh[1])
                        y_dma(y, x2sp[1024:2048, t0:t0 + GA].rearrange(
                            "(m p) t -> p m t", p=128), x2bh[:])

                    @blk.vector
                    def _(v):
                        w(v, "ly")
                        inc(v.nop(), "vm")
                pmark()

        if dbg == "ffn8":
            ffn_phase(xin, x1sp, "f1")
            ffn_phase(x1sp, out, "f2")

            @blk.sync
            def _(y):
                w(y, "ly")
            return nc
        if dbg == "f2solo":
            g_next_sem()
            g_next_sem()
            ffn_phase(xin, out, "f2")

            @blk.sync
            def _(y):
                w(y, "ly")
            return nc
        if dbg in ("x0", "ln"):
            ffn_phase(xin, x1sp, "f1", upto=dbg)

            @blk.sync
            def _(y):
                w(y, "vm")
                y_dma(y, out, x1sp)
                w(y, "ly")
            return nc
        ffn_phase(xin, x1sp, "f1")
        pmark()
        if dbg in ("a1", "a1ln", "a1q", "a1kv"):
            a1_phase({"a1": "full", "a1ln": "ln", "a1q": "q",
                      "a1kv": "kv"}[dbg])

            @blk.sync
            def _(y):
                w(y, "vm")
                y_dma(y, out, qwcsp)
                y_dma(y, dbg_bf, qsp)
                w(y, "ly")
            return nc
        if dbg in ("x1", "a2a"):
            a1_phase()
            x_exchange(0)
            if dbg == "a2a":
                a2a_phase()
                x_exchange(1)

                @blk.sync
                def _(y):
                    w(y, "vm")
                    w(y, "gm")
                    y_dma(y, out, kwcsp)
                    y_dma(y, dbg_bf, mxsp)
                    y_dma(y, dbg_car[:, 0 * ND:1 * ND], c3e[:])
                    y_dma(y, dbg_car[:, 1 * ND:2 * ND], pk_cr[:])
                    w(y, "ly")
                return nc

            @blk.sync
            def _(y):
                w(y, "vm")
                w(y, "gm")
                y_dma(y, out, qwcsp)
                y_dma(y, dbg_bf, qsp)
                y_dma(y, dbg_car[:, 0 * ND:1 * ND], c1e[:])
                y_dma(y, dbg_car[:, 1 * ND:2 * ND], pq_cr[:])
                y_dma(y, dbg_car[:, 2 * ND:3 * ND], qwc_cr[:])
                w(y, "ly")
            return nc
        if dbg == "f1":
            @blk.sync
            def _(y):
                w(y, "vm")
                y_dma(y, out, x1sp)
                w(y, "ly")
        else:
            a1_phase()
            pmark()
            x_exchange(0)
            pmark()
            a2a_phase()
            pmark()
            x_exchange(1)
            pmark()
            a2b_phase({"a2b_l": "loads", "a2b_n": "noscan"}.get(dbg, "full"))
            if dbg in ("a2b", "a2b_l", "a2b_n"):
                @blk.sync
                def _(y):
                    w(y, "vm")
                    y_dma(y, out, x2sp)
                    w(y, "ly")
            else:
                ffn_phase(x2sp, out, "f2")

                @blk.sync
                def _(y):
                    w(y, "ly")
    return nc


def stage_inputs(inputs, core):
    b, half = core // 2, core % 2
    x = np.asarray(inputs["L0_attn_hidden_states"], np.float32)
    xin = np.ascontiguousarray(x[b, half * TC:(half + 1) * TC, :].T)

    def wmat(wname):
        w_ = np.asarray(inputs[wname], np.float32)
        return np.ascontiguousarray(
            w_.T.reshape(ND, 128, 2, 1024).transpose(2, 1, 0, 3))

    def col128(vname):
        v = np.asarray(inputs[vname], np.float32)
        return np.ascontiguousarray(v.reshape(-1, 128).T)

    gwt = np.ascontiguousarray(
        np.asarray(inputs["L0_ffn_gate_w"]).astype(np.uint8).T
        .reshape(ND, 128, NI, 128).transpose(2, 1, 0, 3).reshape(NI, 128, D))
    uwt = np.ascontiguousarray(
        np.asarray(inputs["L0_ffn_up_w"]).astype(np.uint8).T
        .reshape(ND, 128, NI, 128).transpose(2, 1, 0, 3).reshape(NI, 128, D))
    dwt = np.ascontiguousarray(
        np.asarray(inputs["L0_ffn_down_w"]).astype(np.uint8).T)

    return {
        "xin": xin,
        "wq": wmat("L0_attn_Wq"), "wk": wmat("L0_attn_Wk"),
        "wv": wmat("L0_attn_Wv"), "wqa": wmat("L0_attn_Wqa"),
        "wka": wmat("L0_attn_Wka"),
        "gw": gwt, "uw": uwt, "dw": dwt,
        "pgs": col128("L0_ffn_gate_scale"),
        "pus": col128("L0_ffn_up_scale"),
        "pds": col128("L0_ffn_down_scale"),
        "pbq": col128("L0_attn_bq"), "pbk": col128("L0_attn_bk"),
        "pbv": col128("L0_attn_bv"), "pba": col128("L0_attn_bqa"),
        "pbb": col128("L0_attn_bka"),
        "plg": col128("L0_attn_ln_g"), "plb": col128("L0_attn_ln_b"),
        "maskc": np.full((128, 1), float(half), np.float32),
        "slot": np.array([[half]], np.int32),
        "nonce": np.array([[0]], np.int32),
    }


def kernel(**inputs):
    nc = build_nc("full")
    nonce = int(np.random.randint(1, 2 ** 20))
    in_maps = []
    for c in range(8):
        m = stage_inputs(inputs, c)
        m["nonce"] = np.array([[nonce]], np.int32)
        in_maps.append(m)
    res = run_bass_kernel_spmd(nc, in_maps, core_ids=list(range(8)))
    o = np.empty((4, 4096, 2048), np.float32)
    for c in range(8):
        b, half = c // 2, c % 2
        o[b, half * TC:(half + 1) * TC, :] = np.asarray(
            res.results[c]["out"]).T
    return o
